# version 34
# baseline (speedup 1.0000x reference)
import numpy as np
import ml_dtypes
import concourse.bass as bass
import concourse.mybir as mybir
from concourse.bass import broadcast_tensor_aps
from concourse.bass_utils import run_bass_kernel_spmd

F32 = mybir.dt.float32
BF16 = mybir.dt.bfloat16
I32 = mybir.dt.int32
ALU = mybir.AluOpType
AF = mybir.ActivationFunctionType
AX = mybir.AxisListType

NTOK = 4096
D = 1024
HID = 3072
NH = 16
DH = 64
WIN = 256
TILE = 128
NT = NTOK // TILE
HALF_T = 16
S = np.float32(1.0 / 127.0)

W_QKV_OFF = 0
W_OUT_OFF = 49152
W_G_OFF = 0
W_U_OFF = 24576
W_D_OFF = 49152


def _wkeys(off, nbytes):
    return [f"W{b}" for b in range(off // 512, (off + nbytes + 511) // 512)]


class Sched:

    def __init__(self, engines):
        self.engines = engines
        self.ops = {e: [] for e in engines}
        self.cnt = {}
        self.know = {e: {} for e in engines}
        self.snap = {}
        self.buf = {}

    def add(self, eng, fn, reads=(), writes=(), dma=None):
        sem = dma if dma else eng
        amt = 16 if dma else 1
        need = {}

        def req(p):
            s, v = p
            if need.get(s, 0) < v:
                need[s] = v

        for k in reads:
            b = self.buf.get(k)
            if b and b["w"]:
                req(b["w"])
        for k in writes:
            b = self.buf.get(k)
            if b:
                if b["w"]:
                    req(b["w"])
                for r in b["r"]:
                    req(r)
        kn = self.know[eng]
        waits = []
        for s in sorted(need):
            v = need[s]
            if (eng == "t" and s == "t") or (dma is not None and s == dma):
                continue
            if kn.get(s, 0) >= v:
                continue
            waits.append((s, v))
            kn[s] = v
            sn = self.snap.get((s, v))
            if sn:
                for s2, v2 in sn.items():
                    if kn.get(s2, 0) < v2:
                        kn[s2] = v2
        val = self.cnt.get(sem, 0) + amt
        self.cnt[sem] = val
        sn = dict(kn)
        sn[sem] = val
        self.snap[(sem, val)] = sn
        for k in writes:
            self.buf[k] = {"w": (sem, val), "r": []}
        for k in reads:
            b = self.buf.setdefault(k, {"w": None, "r": []})
            b["r"].append((sem, val))
        self.ops[eng].append((waits, fn, sem, amt))

    def fence_writes(self, sem, keys):
        v = self.cnt.get(sem, 0)
        for k in keys:
            self.buf[k] = {"w": (sem, v), "r": []}


def build_program():
    nc = bass.Bass("TRN2", target_bir_lowering=False,
                   detect_race_conditions=False)

    x_d = nc.dram_tensor("x", [NTOK, D], F32, kind="ExternalInput").ap()
    wq_d = nc.dram_tensor("wq", [D, 3 * D], F32, kind="ExternalInput").ap()
    wo_d = nc.dram_tensor("wo", [D, D], F32, kind="ExternalInput").ap()
    wg_d = nc.dram_tensor("wg", [D, HID], I32, kind="ExternalInput").ap()
    wu_d = nc.dram_tensor("wu", [D, HID], I32, kind="ExternalInput").ap()
    wd_d = nc.dram_tensor("wd", [HID, D], I32, kind="ExternalInput").ap()
    id_d = nc.dram_tensor("ident", [TILE, TILE], BF16, kind="ExternalInput").ap()
    mk_d = nc.dram_tensor("masks", [TILE, 3, TILE], BF16, kind="ExternalInput").ap()
    y_d = nc.dram_tensor("y", [NTOK, D], F32, kind="ExternalOutput").ap()

    A = nc.alloc_sbuf_tensor
    yacc = A("yacc", [128, HALF_T, D], BF16)
    xnfm = A("xnfm", [128, 8, HALF_T * TILE], BF16)
    wreg = A("wreg", [128, 73728], mybir.dt.uint8)
    wbase = nc.lookup_mloc(wreg).addr
    AT = nc.alloc_sbuf_tensor_at
    wq_sb = AT("wq_sb", [128, 8, 3 * D], BF16, offset=wbase + W_QKV_OFF)
    wo_sb = AT("wo_sb", [128, 8, D], BF16, offset=wbase + W_OUT_OFF)
    wg_sb = AT("wg_sb", [128, 8, HID // 2], BF16, offset=wbase + W_G_OFF)
    wu_sb = AT("wu_sb", [128, 8, HID // 2], BF16, offset=wbase + W_U_OFF)
    wd_sb = AT("wd_sb", [128, 12, D], BF16, offset=wbase + W_D_OFF)
    xbuf = A("xbuf", [128, 3, D], F32)
    xntm = A("xntm", [128, 2, D], BF16)
    kring = A("kring", [128, 8, 3 * TILE], BF16)
    vring = A("vring", [128, 3, D], BF16)
    qfm = A("qfm", [128, 8, TILE], BF16)
    attnfm = A("attnfm", [128, 8, TILE], BF16)
    eprobs = A("eprobs", [128, 6, 384], BF16)
    mprobs = A("mprobs", [128, 6, 384], BF16)
    probsT = A("probsT", [128, 6, 384], BF16)
    gsb = A("gsb", [128, HID // 2], BF16)
    usb = A("usb", [128, HID // 2], BF16)
    hsb = A("hsb", [128, HID // 2], BF16)
    hfm = A("hfm", [128, 12, TILE], BF16)
    yout = A("yout", [128, 2, D], F32)
    sscr = A("sscr", [128, D], BF16)
    id_sb = A("id_sb", [128, TILE], BF16)
    mk_sb = A("mk_sb", [128, 3, TILE], BF16)
    sumx = A("sumx", [128, 2], F32)
    sumsq = A("sumsq", [128, 2], F32)
    meanb = A("meanb", [128, 2], F32)
    m2b = A("m2b", [128, 2], F32)
    varb = A("varb", [128, 2], F32)
    lnvb = A("lnvb", [128, 2], F32)
    invstd = A("invstd", [128, 2], F32)
    negmb = A("negmb", [128, 2], F32)
    negmax = A("negmax", [128, 6], F32)
    sums = A("sums", [128, 6], F32)
    recip = A("recip", [128, 6], F32)
    epsb = A("epsb", [128, 1], F32)

    P = [nc.alloc_psum_tensor(f"P{i}", [128, 512], F32) for i in range(8)]

    sc = Sched(["t", "v", "s", "g", "q"])
    add = sc.add

    add("q", lambda e: e.dma_start(out=id_sb[:], in_=id_d),
        reads=(), writes=("ident",), dma="dc")
    add("q", lambda e: e.dma_start(out=mk_sb[:], in_=mk_d),
        reads=(), writes=("maskadd",), dma="dc")
    sc.fence_writes("dc", ["ident", "maskadd"])
    add("v", lambda e: e.memset(kring[:], 0.0), writes=("kr0", "kr1", "kr2"))
    add("v", lambda e: e.memset(vring[:], 0.0), writes=("vr0", "vr1", "vr2"))
    add("v", lambda e: e.memset(epsb[:], 1e-5), writes=("eps",))

    def load_attn_weights():
        for grp in range(2):
            keys = []
            for d in range(4 * grp, 4 * grp + 4):
                ks = _wkeys(W_QKV_OFF + d * 6144, 6144)
                keys += ks
                add("g", (lambda d: lambda e: e.dma_start(
                    out=wq_sb[:, d, :], in_=wq_d[d * 128:(d + 1) * 128, :]))(d),
                    writes=ks, dma=f"dwa{grp}")
            sc.fence_writes(f"dwa{grp}", keys)
        keys = []
        for hp in range(8):
            ks = _wkeys(W_OUT_OFF + hp * 2048, 2048)
            keys += ks
            add("g", (lambda hp: lambda e: e.dma_start(
                out=wo_sb[:, hp, :], in_=wo_d[hp * 128:(hp + 1) * 128, :]))(hp),
                writes=ks, dma="dwo")
        sc.fence_writes("dwo", keys)

    def load_ffn_weights(hh):
        for d in range(8):
            ks = _wkeys(W_G_OFF + d * 3072, 3072)
            add("g", (lambda d: lambda e: e.dma_start(
                out=wg_sb[:, d, :],
                in_=wg_d[d * 128:(d + 1) * 128,
                         hh * 1536:(hh + 1) * 1536]))(d),
                writes=ks, dma=f"dwg{d}")
            sc.fence_writes(f"dwg{d}", ks)
            ks = _wkeys(W_U_OFF + d * 3072, 3072)
            add("g", (lambda d: lambda e: e.dma_start(
                out=wu_sb[:, d, :],
                in_=wu_d[d * 128:(d + 1) * 128,
                         hh * 1536:(hh + 1) * 1536]))(d),
                writes=ks, dma=f"dwu{d}")
            sc.fence_writes(f"dwu{d}", ks)
        for grp in range(2):
            kd = []
            for c in range(6 * grp, 6 * grp + 6):
                ks = _wkeys(W_D_OFF + c * 2048, 2048)
                kd += ks
                add("g", (lambda c: lambda e: e.dma_start(
                    out=wd_sb[:, c, :],
                    in_=wd_d[hh * 1536 + c * 128:
                             hh * 1536 + (c + 1) * 128, :]))(c),
                    writes=ks, dma=f"dwd{grp}")
            sc.fence_writes(f"dwd{grp}", kd)

    def dma_x(t):
        xb = t % 3
        add("q", (lambda t, xb: lambda e: e.dma_start(
            out=xbuf[:, xb, :], in_=x_d[t * 128:(t + 1) * 128, :]))(t, xb),
            writes=(f"xb{xb}",), dma=f"dx{xb}")

    def _mchunk(t, c):
        if c == 2:
            return 2
        if t - 2 + c < 0:
            return 0
        return 1 if c == 0 else None

    def pair_scores(t, h0):
        for c in range(3):
            sl = (t - 2 + c) % 3
            mi = _mchunk(t, c)
            for h in (h0, h0 + 1):
                j, hp, pj = h & 1, h >> 1, h % 5
                add("t", (lambda c, sl, j, hp, pj, mi: lambda e: e.matmul(
                    P[pj][:, c * 128:(c + 1) * 128],
                    qfm[64 * j:64 * j + 64, hp, :],
                    kring[64 * j:64 * j + 64, hp, sl * 128:(sl + 1) * 128],
                    start=True, stop=(mi is None)))(c, sl, j, hp, pj, mi),
                    reads=("qfm", f"kr{sl}"), writes=(f"P{pj}",))
            if mi is not None:
                for h in (h0, h0 + 1):
                    pj = h % 5
                    add("t", (lambda c, mi, pj: lambda e: e.matmul(
                        P[pj][:, c * 128:(c + 1) * 128],
                        mk_sb[:, mi, :], id_sb[:],
                        start=False, stop=True))(c, mi, pj),
                        reads=("maskadd", "ident"), writes=(f"P{pj}",))

    def pair_softmax(t, h0):
        for h in (h0, h0 + 1):
            bj, pj = h % 5, h % 6
            add("v", (lambda bj, pj: lambda e: e.reduce_max(
                out=negmax[:, pj:pj + 1], in_=P[bj][:, 0:384], axis=AX.X,
                negate=True))(bj, pj),
                reads=(f"P{bj}",), writes=(f"nm{pj}",))
        for h in (h0, h0 + 1):
            bj, pj = h % 5, h % 6
            add("s", (lambda bj, pj: lambda e: e.activation(
                out=eprobs[:, pj, :], in_=P[bj][:, 0:384], func=AF.Exp,
                bias=negmax[:, pj:pj + 1],
                accum_out=sums[:, pj:pj + 1]))(bj, pj),
                reads=(f"P{bj}", f"nm{pj}"), writes=(f"ep{pj}", f"sm{pj}"))
        pj0 = h0 % 6
        add("v", (lambda pj0: lambda e: e.reciprocal(
            out=recip[:, pj0:pj0 + 2], in_=sums[:, pj0:pj0 + 2]))(pj0),
            reads=(f"sm{pj0}", f"sm{pj0 + 1}"),
            writes=(f"rc{pj0}", f"rc{pj0 + 1}"))
        for h in (h0, h0 + 1):
            pj = h % 6
            ea, rb = broadcast_tensor_aps(eprobs[:, pj, :], recip[:, pj:pj + 1])
            add("g", (lambda pj, ea, rb: lambda e: e.tensor_tensor(
                out=mprobs[:, pj, :], in0=ea, in1=rb, op=ALU.mult))(pj, ea, rb),
                reads=(f"ep{pj}", f"rc{pj}"), writes=(f"mp{pj}",))

    def pair_transpose(t, h0):
        for c in range(3):
            for h in (h0, h0 + 1):
                pj, bk = h % 6, 5 + (h & 1)
                add("t", (lambda c, pj, bk: lambda e: e.matmul(
                    P[bk][:, c * 128:(c + 1) * 128],
                    mprobs[:, pj, c * 128:(c + 1) * 128],
                    id_sb[:], start=True, stop=True))(c, pj, bk),
                    reads=(f"mp{pj}", "ident"), writes=(f"P{bk}",))
        for h in (h0, h0 + 1):
            pj, bk = h % 6, 5 + (h & 1)
            eng = "s" if (h & 1) == 0 else "v"
            if eng == "s":
                add("s", (lambda pj, bk: lambda e: e.activation(
                    out=probsT[:, pj, :], in_=P[bk][:, 0:384],
                    func=AF.Identity))(pj, bk),
                    reads=(f"P{bk}",), writes=(f"pt{pj}",))
            else:
                add("v", (lambda pj, bk: lambda e: e.tensor_copy(
                    out=probsT[:, pj, :], in_=P[bk][:, 0:384]))(pj, bk),
                    reads=(f"P{bk}",), writes=(f"pt{pj}",))

    def pair_av(t, h0):
        hp = h0 >> 1
        avk = 7
        for h in (h0, h0 + 1):
            j, pj = h & 1, h % 6
            for c in range(3):
                sl = (t - 2 + c) % 3
                add("t", (lambda c, sl, j, pj, h, avk: lambda e: e.matmul(
                    P[avk][64 * j:64 * j + 64, 0:128],
                    vring[:, sl, h * 64:(h + 1) * 64],
                    probsT[:, pj, c * 128:(c + 1) * 128],
                    start=(c == 0), stop=(c == 2),
                    tile_position=(0, 64 * j)))(c, sl, j, pj, h, avk),
                    reads=(f"vr{sl}", f"pt{pj}"), writes=(f"P{avk}",))
        add("v", (lambda hp, avk: lambda e: e.tensor_copy(
            out=attnfm[:, hp, :], in_=P[avk][:, 0:128]))(hp, avk),
            reads=(f"P{avk}",), writes=(f"af{hp}",))

    def ln_block(t):
        ti, xb = t % HALF_T, t % 3
        xnb = t % 2
        add("s", (lambda xb, ti: lambda e: e.activation(
            out=sscr[:], in_=xbuf[:, xb, :], func=AF.Square,
            accum_out=sumsq[:, ti % 2:ti % 2 + 1]))(xb, ti),
            reads=(f"xb{xb}",), writes=("sscr", f"sq{ti % 2}"))
        add("v", (lambda xb, ti: lambda e: e.reduce_sum(
            out=sumx[:, ti % 2:ti % 2 + 1], in_=xbuf[:, xb, :],
            axis=AX.X))(xb, ti),
            reads=(f"xb{xb}",), writes=(f"sx{ti % 2}",))
        add("v", (lambda ti: lambda e: e.tensor_scalar(
            out=meanb[:, ti % 2:ti % 2 + 1], in0=sumx[:, ti % 2:ti % 2 + 1],
            scalar1=1.0 / D, scalar2=None, op0=ALU.mult))(ti),
            reads=(f"sx{ti % 2}",), writes=(f"mn{ti % 2}",))
        add("v", (lambda ti: lambda e: e.tensor_mul(
            out=m2b[:, ti % 2:ti % 2 + 1], in0=meanb[:, ti % 2:ti % 2 + 1],
            in1=meanb[:, ti % 2:ti % 2 + 1]))(ti),
            reads=(f"mn{ti % 2}",), writes=(f"m2{ti % 2}",))
        add("v", (lambda ti: lambda e: e.scalar_tensor_tensor(
            out=varb[:, ti % 2:ti % 2 + 1], in0=sumsq[:, ti % 2:ti % 2 + 1],
            scalar=1.0 / D, in1=m2b[:, ti % 2:ti % 2 + 1],
            op0=ALU.mult, op1=ALU.subtract))(ti),
            reads=(f"sq{ti % 2}", f"m2{ti % 2}"), writes=(f"vr_{ti % 2}",))
        add("s", (lambda ti: lambda e: e.activation(
            out=lnvb[:, ti % 2:ti % 2 + 1], in_=varb[:, ti % 2:ti % 2 + 1],
            func=AF.Ln, bias=epsb[:, 0:1]))(ti),
            reads=(f"vr_{ti % 2}", "eps"), writes=(f"lv{ti % 2}",))
        add("s", (lambda ti: lambda e: e.activation(
            out=invstd[:, ti % 2:ti % 2 + 1], in_=lnvb[:, ti % 2:ti % 2 + 1],
            func=AF.Exp, scale=-0.5))(ti),
            reads=(f"lv{ti % 2}",), writes=(f"iv{ti % 2}",))
        add("v", (lambda ti: lambda e: e.scalar_tensor_tensor(
            out=negmb[:, ti % 2:ti % 2 + 1], in0=meanb[:, ti % 2:ti % 2 + 1],
            scalar=-1.0, in1=invstd[:, ti % 2:ti % 2 + 1],
            op0=ALU.mult, op1=ALU.mult))(ti),
            reads=(f"mn{ti % 2}", f"iv{ti % 2}"), writes=(f"nb{ti % 2}",))
        add("s", (lambda xb, xnb, ti: lambda e: e.activation(
            out=xntm[:, xnb, :], in_=xbuf[:, xb, :], func=AF.Identity,
            bias=negmb[:, ti % 2:ti % 2 + 1],
            scale=invstd[:, ti % 2:ti % 2 + 1]))(xb, xnb, ti),
            reads=(f"xb{xb}", f"nb{ti % 2}", f"iv{ti % 2}"),
            writes=(f"xn{xnb}",))

    def ln_xpose(t):
        ti, xnb = t % HALF_T, t % 2
        for c in range(8):
            pb = P[6 + c // 4]
            add("t", (lambda c, pb, xnb: lambda e: e.matmul(
                pb[:, (c % 4) * 128:(c % 4 + 1) * 128],
                xntm[:, xnb, c * 128:(c + 1) * 128], id_sb[:],
                start=True, stop=True))(c, pb, xnb),
                reads=(f"xn{xnb}", "ident"), writes=(f"P{6 + c // 4}",))
        add("v", (lambda ti: lambda e: e.tensor_copy(
            out=xnfm[:, 0:4, ti * 128:(ti + 1) * 128],
            in_=P[6][:].rearrange("p (c f) -> p c f", f=128)))(ti),
            reads=("P6",), writes=(f"xf{ti}",))
        add("s", (lambda ti: lambda e: e.activation(
            out=xnfm[:, 4:8, ti * 128:(ti + 1) * 128],
            in_=P[7][:].rearrange("p (c f) -> p c f", f=128),
            func=AF.Identity))(ti),
            reads=("P7",), writes=(f"xf{ti}",))

    def phase_a_tile(t):
        ti, xb = t % HALF_T, t % 3
        slot = t % 3
        if (t + 2) % HALF_T > 1 and t + 2 < NT:
            dma_x(t + 2)
        if (t + 1) % HALF_T != 0 and t + 1 < NT:
            ln_block(t + 1)
        for fc in range(16):
            bank = fc // 4
            for d in range(8):
                add("t", (lambda fc, d, bank, ti: lambda e: e.matmul(
                    P[bank][:, (fc % 4) * 128:(fc % 4 + 1) * 128],
                    wq_sb[:, d, fc * 128:(fc + 1) * 128],
                    xnfm[:, d, ti * 128:(ti + 1) * 128],
                    start=(d == 0), stop=(d == 7)))(fc, d, bank, ti),
                    reads=tuple(_wkeys(W_QKV_OFF + d * 6144 + fc * 256, 256))
                    + (f"xf{ti}",),
                    writes=(f"P{bank}",))
            if fc % 4 == 3:
                if fc < 8:
                    add("v", (lambda bank, fc: lambda e: e.tensor_scalar(
                        out=qfm[:, bank * 4:bank * 4 + 4, :],
                        in0=P[bank][:].rearrange("p (c f) -> p c f", f=128),
                        scalar1=0.125, scalar2=None, op0=ALU.mult))(bank, fc),
                        reads=(f"P{bank}",), writes=("qfm",))
                else:
                    kb = bank - 2
                    add("v", (lambda bank, kb, slot: lambda e: e.tensor_copy(
                        out=kring[:, kb * 4:kb * 4 + 4,
                                  slot * 128:(slot + 1) * 128],
                        in_=P[bank][:].rearrange("p (c f) -> p c f", f=128)))
                        (bank, kb, slot),
                        reads=(f"P{bank}",), writes=(f"kr{slot}",))
        for d in range(8):
            for nb in range(2):
                add("t", (lambda d, nb, ti: lambda e: e.matmul(
                    P[4 + nb][:, :],
                    xnfm[:, d, ti * 128:(ti + 1) * 128],
                    wq_sb[:, d, 2048 + nb * 512:2048 + (nb + 1) * 512],
                    start=(d == 0), stop=(d == 7)))(d, nb, ti),
                    reads=tuple(_wkeys(W_QKV_OFF + d * 6144 + 4096 + nb * 1024,
                                       1024)) + (f"xf{ti}",),
                    writes=(f"P{4 + nb}",))
        for nb in range(2):
            add("s", (lambda nb, slot: lambda e: e.activation(
                out=vring[:, slot, nb * 512:(nb + 1) * 512],
                in_=P[4 + nb][:], func=AF.Identity))(nb, slot),
                reads=(f"P{4 + nb}",), writes=(f"vr{slot}",))
        if (t + 1) % HALF_T != 0 and t + 1 < NT:
            ln_xpose(t + 1)
        NP = NH // 2
        op_sched = {8: (0, 1), 9: (2, 3), 10: (4, 5), 11: (6,), 12: (7,)}

        def outproj(hp):
            for nb in range(2):
                add("t", (lambda hp, nb: lambda e: e.matmul(
                    P[2 + nb][:, :], attnfm[:, hp, :],
                    wo_sb[:, hp, nb * 512:(nb + 1) * 512],
                    start=(hp == 0), stop=(hp == 7)))(hp, nb),
                    reads=(f"af{hp}",)
                    + tuple(_wkeys(W_OUT_OFF + hp * 2048 + nb * 1024, 1024)),
                    writes=(f"P{2 + nb}",))

        for s_ in range(NP + 5):
            if 1 <= s_ <= NP:
                pair_softmax(t, 2 * (s_ - 1))
            if s_ < NP:
                pair_scores(t, 2 * s_)
            if 3 <= s_ <= NP + 2:
                pair_transpose(t, 2 * (s_ - 3))
            if 4 <= s_ <= NP + 3:
                pair_av(t, 2 * (s_ - 4))
            for hp in op_sched.get(s_, ()):
                outproj(hp)
        for nb in range(2):
            add("v", (lambda nb, ti, xb: lambda e: e.scalar_tensor_tensor(
                out=yacc[:, ti, nb * 512:(nb + 1) * 512], in0=P[2 + nb][:],
                scalar=1.0, in1=xbuf[:, xb, nb * 512:(nb + 1) * 512],
                op0=ALU.mult, op1=ALU.add))(nb, ti, xb),
                reads=(f"P{2 + nb}", f"xb{xb}"), writes=(f"ya{ti}",))

    def phase_b_parts(hh, ti, last):
        def gate_up(nb):
            gb, ub = nb % 2, 2 + nb % 2
            for d in range(8):
                add("t", (lambda d, nb, ti, gb: lambda e: e.matmul(
                    P[gb][:, :], xnfm[:, d, ti * 128:(ti + 1) * 128],
                    wg_sb[:, d, nb * 512:(nb + 1) * 512],
                    start=(d == 0), stop=(d == 7)))(d, nb, ti, gb),
                    reads=tuple(_wkeys(W_G_OFF + d * 3072 + nb * 1024, 1024))
                    + (f"xf{ti}",),
                    writes=(f"P{gb}",))
                add("t", (lambda d, nb, ti, ub: lambda e: e.matmul(
                    P[ub][:, :], xnfm[:, d, ti * 128:(ti + 1) * 128],
                    wu_sb[:, d, nb * 512:(nb + 1) * 512],
                    start=(d == 0), stop=(d == 7)))(d, nb, ti, ub),
                    reads=tuple(_wkeys(W_U_OFF + d * 3072 + nb * 1024, 1024))
                    + (f"xf{ti}",),
                    writes=(f"P{ub}",))
            add("s", (lambda nb, gb: lambda e: e.activation(
                out=gsb[:, nb * 512:(nb + 1) * 512], in_=P[gb][:],
                func=AF.Silu, scale=1.0 / 127.0))(nb, gb),
                reads=(f"P{gb}",), writes=(f"gs{nb}",))
            add("v", (lambda nb, ub: lambda e: e.tensor_scalar(
                out=usb[:, nb * 512:(nb + 1) * 512], in0=P[ub][:],
                scalar1=1.0 / 127.0, scalar2=None, op0=ALU.mult))(nb, ub),
                reads=(f"P{ub}",), writes=(f"us{nb}",))
            add("v", (lambda nb: lambda e: e.tensor_mul(
                out=hsb[:, nb * 512:(nb + 1) * 512],
                in0=gsb[:, nb * 512:(nb + 1) * 512],
                in1=usb[:, nb * 512:(nb + 1) * 512]))(nb),
                reads=(f"gs{nb}", f"us{nb}"), writes=(f"hs{nb}",))

        def h_trans(nb):
            tb = 4 + nb % 2
            for c4 in range(4):
                c = 4 * nb + c4
                add("t", (lambda c, c4, nb, tb: lambda e: e.matmul(
                    P[tb][:, c4 * 128:(c4 + 1) * 128],
                    hsb[:, c * 128:(c + 1) * 128], id_sb[:],
                    start=True, stop=True))(c, c4, nb, tb),
                    reads=(f"hs{nb}", "ident"), writes=(f"P{tb}",))
            add("s", (lambda nb, tb: lambda e: e.activation(
                out=hfm[:, 4 * nb:4 * nb + 4, :],
                in_=P[tb][:].rearrange("p (c f) -> p c f", f=128),
                func=AF.Identity))(nb, tb),
                reads=(f"P{tb}",), writes=(f"hf{nb}",))

        def down(nb):
            for c4 in range(4):
                c = 4 * nb + c4
                for n2 in range(2):
                    add("t", (lambda c, nb, n2: lambda e: e.matmul(
                        P[6 + n2][:, :], hfm[:, c, :],
                        wd_sb[:, c, n2 * 512:(n2 + 1) * 512],
                        start=(c == 0), stop=(c == 11)))(c, nb, n2),
                        reads=(f"hf{nb}",)
                        + tuple(_wkeys(W_D_OFF + c * 2048 + n2 * 1024, 1024)),
                        writes=(f"P{6 + n2}",))
            if nb == 2:
                for n2 in range(2):
                    if not last:
                        add("v", (lambda n2, ti: lambda e:
                            e.scalar_tensor_tensor(
                                out=yacc[:, ti, n2 * 512:(n2 + 1) * 512],
                                in0=P[6 + n2][:], scalar=1.0 / 127.0,
                                in1=yacc[:, ti, n2 * 512:(n2 + 1) * 512],
                                op0=ALU.mult, op1=ALU.add))(n2, ti),
                            reads=(f"P{6 + n2}", f"ya{ti}"),
                            writes=(f"ya{ti}",))
                    else:
                        yb = ti % 2
                        add("v", (lambda n2, ti, yb: lambda e:
                            e.scalar_tensor_tensor(
                                out=yout[:, yb, n2 * 512:(n2 + 1) * 512],
                                in0=P[6 + n2][:], scalar=1.0 / 127.0,
                                in1=yacc[:, ti, n2 * 512:(n2 + 1) * 512],
                                op0=ALU.mult, op1=ALU.add))(n2, ti, yb),
                            reads=(f"P{6 + n2}", f"ya{ti}"),
                            writes=(f"yo{yb}",))
                if last:
                    t = _half[0] * HALF_T + ti
                    yb = ti % 2
                    add("q", (lambda t, yb: lambda e: e.dma_start(
                        out=y_d[t * 128:(t + 1) * 128, :],
                        in_=yout[:, yb, :]))(t, yb),
                        reads=(f"yo{yb}",), dma=f"dy{yb}")

        return gate_up, h_trans, down

    def phase_b(hh, last):
        nsteps = 3 * HALF_T
        parts = [phase_b_parts(hh, ti, last) for ti in range(HALF_T)]
        for st_ in range(nsteps + 2):
            if st_ < nsteps:
                ti, nb = divmod(st_, 3)
                parts[ti][0](nb)
            if st_ >= 1 and st_ - 1 < nsteps:
                ti, nb = divmod(st_ - 1, 3)
                parts[ti][1](nb)
            if st_ >= 2 and st_ - 2 < nsteps:
                ti, nb = divmod(st_ - 2, 3)
                parts[ti][2](nb)

    _half = [0]
    for H in range(2):
        _half[0] = H
        load_attn_weights()
        dma_x(H * HALF_T)
        dma_x(H * HALF_T + 1)
        ln_block(H * HALF_T)
        ln_xpose(H * HALF_T)
        for ti in range(HALF_T):
            phase_a_tile(H * HALF_T + ti)
        for hh in range(2):
            load_ffn_weights(hh)
            phase_b(hh, last=(hh == 1))

    from contextlib import ExitStack
    with ExitStack() as ctx:
        sems = {}
        for name in ["t", "v", "s", "g", "dc", "dx0", "dx1", "dx2",
                     "dwa0", "dwa1", "dwo",
                     "dwg0", "dwg1", "dwg2", "dwg3",
                     "dwg4", "dwg5", "dwg6", "dwg7",
                     "dwu0", "dwu1", "dwu2", "dwu3",
                     "dwu4", "dwu5", "dwu6", "dwu7",
                     "dwd0", "dwd1", "dy0", "dy1"]:
            sems[name] = ctx.enter_context(nc.semaphore(name))
        block = ctx.enter_context(nc.Block())

        def emit(ename):
            def run(eng):
                from concourse.bass import inst_waits_full
                for waits, fn, sem, amt in sc.ops[ename]:
                    att = waits[-1:] if waits else []
                    for s_, v_ in (waits[:-1] if att else waits):
                        eng.wait_ge(sems[s_], v_)
                    ins = fn(eng)
                    for s_, v_ in att:
                        if inst_waits_full(ins.ins):
                            raise RuntimeError("sync header full")
                        ins._wait_ge(sems[s_], v_)
                    ins.then_inc(sems[sem], amt)
                if ename == "q":
                    eng.wait_ge(sems["dy0"], sc.cnt["dy0"])
                    eng.wait_ge(sems["dy1"], sc.cnt["dy1"])
            return run

        block.tensor(emit("t"))
        block.vector(emit("v"))
        block.scalar(emit("s"))
        block.gpsimd(emit("g"))
        block.sync(emit("q"))
    return nc


def _consts():
    ident = np.eye(TILE, dtype=ml_dtypes.bfloat16)
    p = np.arange(TILE)[:, None]
    c = np.arange(TILE)[None, :]
    neg = np.float32(-1000.0)
    m_full = np.full((TILE, TILE), neg, dtype=np.float32)
    m_a = np.where(c <= p, neg, 0.0).astype(np.float32)
    m_c = np.where(c > p, neg, 0.0).astype(np.float32)
    masks = np.stack([m_full.T, m_a.T, m_c.T], axis=0)
    masks = np.ascontiguousarray(
        masks.transpose(1, 0, 2)).astype(ml_dtypes.bfloat16)
    return ident, masks


_NC_CACHE = {}


def kernel(L0_attn_x, L0_attn_w_qkv, L0_attn_w_out, L0_ffn_w_gate,
           L0_ffn_w_up, L0_ffn_w_down):
    x = np.ascontiguousarray(np.asarray(L0_attn_x), dtype=np.float32)
    wq = np.ascontiguousarray(np.asarray(L0_attn_w_qkv), dtype=np.float32)
    wo = np.ascontiguousarray(np.asarray(L0_attn_w_out), dtype=np.float32)
    wg = np.ascontiguousarray(np.asarray(L0_ffn_w_gate).T).astype(np.int32)
    wu = np.ascontiguousarray(np.asarray(L0_ffn_w_up).T).astype(np.int32)
    wd = np.ascontiguousarray(np.asarray(L0_ffn_w_down).T).astype(np.int32)
    ident, masks = _consts()
    if "nc" not in _NC_CACHE:
        _NC_CACHE["nc"] = build_program()
    nc = _NC_CACHE["nc"]
    in_maps = []
    for cidx in range(8):
        in_maps.append({
            "x": np.ascontiguousarray(x[cidx]),
            "wq": wq, "wo": wo, "wg": wg, "wu": wu, "wd": wd,
            "ident": ident, "masks": masks,
        })
    res = run_bass_kernel_spmd(nc, in_maps, core_ids=list(range(8)))
    out = np.stack([res.results[cidx]["y"] for cidx in range(8)], axis=0)
    return out


# version 35 (final)
# speedup vs baseline: 1.0288x; 1.0288x over previous
import numpy as np
import ml_dtypes
import concourse.bass as bass
import concourse.mybir as mybir
from concourse.bass import broadcast_tensor_aps
from concourse.bass_utils import run_bass_kernel_spmd

F32 = mybir.dt.float32
BF16 = mybir.dt.bfloat16
I32 = mybir.dt.int32
ALU = mybir.AluOpType
AF = mybir.ActivationFunctionType
AX = mybir.AxisListType

NTOK = 4096
D = 1024
HID = 3072
NH = 16
DH = 64
WIN = 256
TILE = 128
NT = NTOK // TILE
HALF_T = 16
S = np.float32(1.0 / 127.0)

W_QKV_OFF = 0
W_OUT_OFF = 49152
W_G_OFF = 0
W_U_OFF = 24576
W_D_OFF = 49152


def _wkeys(off, nbytes):
    return [f"W{b}" for b in range(off // 512, (off + nbytes + 511) // 512)]


class Sched:

    def __init__(self, engines):
        self.engines = engines
        self.ops = {e: [] for e in engines}
        self.cnt = {}
        self.know = {e: {} for e in engines}
        self.snap = {}
        self.buf = {}

    def add(self, eng, fn, reads=(), writes=(), dma=None):
        sem = dma if dma else eng
        amt = 16 if dma else 1
        need = {}

        def req(p):
            s, v = p
            if need.get(s, 0) < v:
                need[s] = v

        for k in reads:
            b = self.buf.get(k)
            if b and b["w"]:
                req(b["w"])
        for k in writes:
            b = self.buf.get(k)
            if b:
                if b["w"]:
                    req(b["w"])
                for r in b["r"]:
                    req(r)
        kn = self.know[eng]
        waits = []
        for s in sorted(need):
            v = need[s]
            if (eng == "t" and s == "t") or (dma is not None and s == dma):
                continue
            if kn.get(s, 0) >= v:
                continue
            waits.append((s, v))
            kn[s] = v
            sn = self.snap.get((s, v))
            if sn:
                for s2, v2 in sn.items():
                    if kn.get(s2, 0) < v2:
                        kn[s2] = v2
        val = self.cnt.get(sem, 0) + amt
        self.cnt[sem] = val
        sn = dict(kn)
        sn[sem] = val
        self.snap[(sem, val)] = sn
        for k in writes:
            self.buf[k] = {"w": (sem, val), "r": []}
        for k in reads:
            b = self.buf.setdefault(k, {"w": None, "r": []})
            b["r"].append((sem, val))
        self.ops[eng].append((waits, fn, sem, amt))

    def fence_writes(self, sem, keys):
        v = self.cnt.get(sem, 0)
        for k in keys:
            self.buf[k] = {"w": (sem, v), "r": []}


def build_program():
    nc = bass.Bass("TRN2", target_bir_lowering=False,
                   detect_race_conditions=False)

    x_d = nc.dram_tensor("x", [NTOK, D], F32, kind="ExternalInput").ap()
    wq_d = nc.dram_tensor("wq", [D, 3 * D], F32, kind="ExternalInput").ap()
    wo_d = nc.dram_tensor("wo", [D, D], F32, kind="ExternalInput").ap()
    wg_d = nc.dram_tensor("wg", [D, HID], I32, kind="ExternalInput").ap()
    wu_d = nc.dram_tensor("wu", [D, HID], I32, kind="ExternalInput").ap()
    wd_d = nc.dram_tensor("wd", [HID, D], I32, kind="ExternalInput").ap()
    id_d = nc.dram_tensor("ident", [TILE, TILE], BF16, kind="ExternalInput").ap()
    mk_d = nc.dram_tensor("masks", [TILE, 3, TILE], BF16, kind="ExternalInput").ap()
    y_d = nc.dram_tensor("y", [NTOK, D], F32, kind="ExternalOutput").ap()

    A = nc.alloc_sbuf_tensor
    yacc = A("yacc", [128, HALF_T, D], BF16)
    xnfm = A("xnfm", [128, 8, HALF_T * TILE], BF16)
    wreg = A("wreg", [128, 73728], mybir.dt.uint8)
    wbase = nc.lookup_mloc(wreg).addr
    AT = nc.alloc_sbuf_tensor_at
    wq_sb = AT("wq_sb", [128, 8, 3 * D], BF16, offset=wbase + W_QKV_OFF)
    wo_sb = AT("wo_sb", [128, 8, D], BF16, offset=wbase + W_OUT_OFF)
    wg_sb = AT("wg_sb", [128, 8, HID // 2], BF16, offset=wbase + W_G_OFF)
    wu_sb = AT("wu_sb", [128, 8, HID // 2], BF16, offset=wbase + W_U_OFF)
    wd_sb = AT("wd_sb", [128, 12, D], BF16, offset=wbase + W_D_OFF)
    xbuf = A("xbuf", [128, 3, D], F32)
    xntm = A("xntm", [128, 2, D], BF16)
    kring = A("kring", [128, 8, 3 * TILE], BF16)
    vring = A("vring", [128, 3, D], BF16)
    qfm = A("qfm", [128, 8, TILE], BF16)
    attnfm = A("attnfm", [128, 8, TILE], BF16)
    eprobs = A("eprobs", [128, 6, 384], BF16)
    mprobs = A("mprobs", [128, 6, 384], BF16)
    probsT = A("probsT", [128, 6, 384], BF16)
    gsb = A("gsb", [128, HID // 2], BF16)
    usb = A("usb", [128, HID // 2], BF16)
    hsb = A("hsb", [128, HID // 2], BF16)
    hfm = A("hfm", [128, 12, TILE], BF16)
    yout = A("yout", [128, 2, D], F32)
    sscr = A("sscr", [128, D], BF16)
    id_sb = A("id_sb", [128, TILE], BF16)
    mk_sb = A("mk_sb", [128, 3, TILE], BF16)
    sumx = A("sumx", [128, 2], F32)
    sumsq = A("sumsq", [128, 2], F32)
    meanb = A("meanb", [128, 2], F32)
    m2b = A("m2b", [128, 2], F32)
    varb = A("varb", [128, 2], F32)
    lnvb = A("lnvb", [128, 2], F32)
    invstd = A("invstd", [128, 2], F32)
    negmb = A("negmb", [128, 2], F32)
    negmax = A("negmax", [128, 6], F32)
    sums = A("sums", [128, 6], F32)
    recip = A("recip", [128, 6], F32)
    epsb = A("epsb", [128, 1], F32)

    P = [nc.alloc_psum_tensor(f"P{i}", [128, 512], F32) for i in range(8)]

    sc = Sched(["t", "v", "s", "g", "q"])
    add = sc.add

    add("q", lambda e: e.dma_start(out=id_sb[:], in_=id_d),
        reads=(), writes=("ident",), dma="dc")
    add("q", lambda e: e.dma_start(out=mk_sb[:], in_=mk_d),
        reads=(), writes=("maskadd",), dma="dc")
    sc.fence_writes("dc", ["ident", "maskadd"])
    add("v", lambda e: e.memset(kring[:], 0.0), writes=("kr0", "kr1", "kr2"))
    add("v", lambda e: e.memset(vring[:], 0.0), writes=("vr0", "vr1", "vr2"))
    add("v", lambda e: e.memset(epsb[:], 1e-5), writes=("eps",))

    def load_attn_weights():
        for grp in range(2):
            keys = []
            for d in range(4 * grp, 4 * grp + 4):
                ks = _wkeys(W_QKV_OFF + d * 6144, 6144)
                keys += ks
                add("g", (lambda d: lambda e: e.dma_start(
                    out=wq_sb[:, d, :], in_=wq_d[d * 128:(d + 1) * 128, :]))(d),
                    writes=ks, dma=f"dwa{grp}")
            sc.fence_writes(f"dwa{grp}", keys)
        keys = []
        for hp in range(8):
            ks = _wkeys(W_OUT_OFF + hp * 2048, 2048)
            keys += ks
            add("g", (lambda hp: lambda e: e.dma_start(
                out=wo_sb[:, hp, :], in_=wo_d[hp * 128:(hp + 1) * 128, :]))(hp),
                writes=ks, dma="dwo")
        sc.fence_writes("dwo", keys)

    def load_ffn_weights(hh):
        for d in range(8):
            ks = _wkeys(W_G_OFF + d * 3072, 3072)
            add("g", (lambda d: lambda e: e.dma_start(
                out=wg_sb[:, d, :],
                in_=wg_d[d * 128:(d + 1) * 128,
                         hh * 1536:(hh + 1) * 1536]))(d),
                writes=ks, dma=f"dwg{d}")
            sc.fence_writes(f"dwg{d}", ks)
            ks = _wkeys(W_U_OFF + d * 3072, 3072)
            add("g", (lambda d: lambda e: e.dma_start(
                out=wu_sb[:, d, :],
                in_=wu_d[d * 128:(d + 1) * 128,
                         hh * 1536:(hh + 1) * 1536]))(d),
                writes=ks, dma=f"dwu{d}")
            sc.fence_writes(f"dwu{d}", ks)
        for grp in range(2):
            kd = []
            for c in range(6 * grp, 6 * grp + 6):
                ks = _wkeys(W_D_OFF + c * 2048, 2048)
                kd += ks
                add("g", (lambda c: lambda e: e.dma_start(
                    out=wd_sb[:, c, :],
                    in_=wd_d[hh * 1536 + c * 128:
                             hh * 1536 + (c + 1) * 128, :]))(c),
                    writes=ks, dma=f"dwd{grp}")
            sc.fence_writes(f"dwd{grp}", kd)

    def dma_x(t):
        xb = t % 3
        add("q", (lambda t, xb: lambda e: e.dma_start(
            out=xbuf[:, xb, :], in_=x_d[t * 128:(t + 1) * 128, :]))(t, xb),
            writes=(f"xb{xb}",), dma=f"dx{xb}")

    def _mchunk(t, c):
        if c == 2:
            return 2
        if t - 2 + c < 0:
            return 0
        return 1 if c == 0 else None

    def pair_scores(t, h0):
        for c in range(3):
            sl = (t - 2 + c) % 3
            mi = _mchunk(t, c)
            for h in (h0, h0 + 1):
                j, hp, pj = h & 1, h >> 1, h % 5
                add("t", (lambda c, sl, j, hp, pj, mi: lambda e: e.matmul(
                    P[pj][:, c * 128:(c + 1) * 128],
                    qfm[64 * j:64 * j + 64, hp, :],
                    kring[64 * j:64 * j + 64, hp, sl * 128:(sl + 1) * 128],
                    start=True, stop=(mi is None)))(c, sl, j, hp, pj, mi),
                    reads=("qfm", f"kr{sl}"), writes=(f"P{pj}",))
            if mi is not None:
                for h in (h0, h0 + 1):
                    pj = h % 5
                    add("t", (lambda c, mi, pj: lambda e: e.matmul(
                        P[pj][:, c * 128:(c + 1) * 128],
                        mk_sb[:, mi, :], id_sb[:],
                        start=False, stop=True))(c, mi, pj),
                        reads=("maskadd", "ident"), writes=(f"P{pj}",))

    def pair_softmax(t, h0):
        for h in (h0, h0 + 1):
            bj, pj = h % 5, h % 6
            add("v", (lambda bj, pj: lambda e: e.reduce_max(
                out=negmax[:, pj:pj + 1], in_=P[bj][:, 0:384], axis=AX.X,
                negate=True))(bj, pj),
                reads=(f"P{bj}",), writes=(f"nm{pj}",))
        for h in (h0, h0 + 1):
            bj, pj = h % 5, h % 6
            add("s", (lambda bj, pj: lambda e: e.activation(
                out=eprobs[:, pj, :], in_=P[bj][:, 0:384], func=AF.Exp,
                bias=negmax[:, pj:pj + 1],
                accum_out=sums[:, pj:pj + 1]))(bj, pj),
                reads=(f"P{bj}", f"nm{pj}"), writes=(f"ep{pj}", f"sm{pj}"))
        pj0 = h0 % 6
        add("v", (lambda pj0: lambda e: e.reciprocal(
            out=recip[:, pj0:pj0 + 2], in_=sums[:, pj0:pj0 + 2]))(pj0),
            reads=(f"sm{pj0}", f"sm{pj0 + 1}"),
            writes=(f"rc{pj0}", f"rc{pj0 + 1}"))
        for h in (h0, h0 + 1):
            pj = h % 6
            ea, rb = broadcast_tensor_aps(eprobs[:, pj, :], recip[:, pj:pj + 1])
            add("g", (lambda pj, ea, rb: lambda e: e.tensor_tensor(
                out=mprobs[:, pj, :], in0=ea, in1=rb, op=ALU.mult))(pj, ea, rb),
                reads=(f"ep{pj}", f"rc{pj}"), writes=(f"mp{pj}",))

    def pair_transpose(t, h0):
        for c in range(3):
            for h in (h0, h0 + 1):
                pj, bk = h % 6, 5 + (h & 1)
                add("t", (lambda c, pj, bk: lambda e: e.matmul(
                    P[bk][:, c * 128:(c + 1) * 128],
                    mprobs[:, pj, c * 128:(c + 1) * 128],
                    id_sb[:], start=True, stop=True))(c, pj, bk),
                    reads=(f"mp{pj}", "ident"), writes=(f"P{bk}",))
        for h in (h0, h0 + 1):
            pj, bk = h % 6, 5 + (h & 1)
            eng = "s" if (h & 1) == 0 else "v"
            if eng == "s":
                add("s", (lambda pj, bk: lambda e: e.activation(
                    out=probsT[:, pj, :], in_=P[bk][:, 0:384],
                    func=AF.Identity))(pj, bk),
                    reads=(f"P{bk}",), writes=(f"pt{pj}",))
            else:
                add("v", (lambda pj, bk: lambda e: e.tensor_copy(
                    out=probsT[:, pj, :], in_=P[bk][:, 0:384]))(pj, bk),
                    reads=(f"P{bk}",), writes=(f"pt{pj}",))

    def pair_av(t, h0):
        hp = h0 >> 1
        avk = 7
        for h in (h0, h0 + 1):
            j, pj = h & 1, h % 6
            for c in range(3):
                sl = (t - 2 + c) % 3
                add("t", (lambda c, sl, j, pj, h, avk: lambda e: e.matmul(
                    P[avk][64 * j:64 * j + 64, 0:128],
                    vring[:, sl, h * 64:(h + 1) * 64],
                    probsT[:, pj, c * 128:(c + 1) * 128],
                    start=(c == 0), stop=(c == 2),
                    tile_position=(0, 64 * j)))(c, sl, j, pj, h, avk),
                    reads=(f"vr{sl}", f"pt{pj}"), writes=(f"P{avk}",))
        add("v", (lambda hp, avk: lambda e: e.tensor_copy(
            out=attnfm[:, hp, :], in_=P[avk][:, 0:128]))(hp, avk),
            reads=(f"P{avk}",), writes=(f"af{hp}",))

    def ln_block(t):
        ti, xb = t % HALF_T, t % 3
        xnb = t % 2
        add("s", (lambda xb, ti: lambda e: e.activation(
            out=sscr[:], in_=xbuf[:, xb, :], func=AF.Square,
            accum_out=sumsq[:, ti % 2:ti % 2 + 1]))(xb, ti),
            reads=(f"xb{xb}",), writes=("sscr", f"sq{ti % 2}"))
        add("v", (lambda xb, ti: lambda e: e.reduce_sum(
            out=sumx[:, ti % 2:ti % 2 + 1], in_=xbuf[:, xb, :],
            axis=AX.X))(xb, ti),
            reads=(f"xb{xb}",), writes=(f"sx{ti % 2}",))
        add("v", (lambda ti: lambda e: e.tensor_scalar(
            out=meanb[:, ti % 2:ti % 2 + 1], in0=sumx[:, ti % 2:ti % 2 + 1],
            scalar1=1.0 / D, scalar2=None, op0=ALU.mult))(ti),
            reads=(f"sx{ti % 2}",), writes=(f"mn{ti % 2}",))
        add("v", (lambda ti: lambda e: e.tensor_mul(
            out=m2b[:, ti % 2:ti % 2 + 1], in0=meanb[:, ti % 2:ti % 2 + 1],
            in1=meanb[:, ti % 2:ti % 2 + 1]))(ti),
            reads=(f"mn{ti % 2}",), writes=(f"m2{ti % 2}",))
        add("v", (lambda ti: lambda e: e.scalar_tensor_tensor(
            out=varb[:, ti % 2:ti % 2 + 1], in0=sumsq[:, ti % 2:ti % 2 + 1],
            scalar=1.0 / D, in1=m2b[:, ti % 2:ti % 2 + 1],
            op0=ALU.mult, op1=ALU.subtract))(ti),
            reads=(f"sq{ti % 2}", f"m2{ti % 2}"), writes=(f"vr_{ti % 2}",))
        add("s", (lambda ti: lambda e: e.activation(
            out=lnvb[:, ti % 2:ti % 2 + 1], in_=varb[:, ti % 2:ti % 2 + 1],
            func=AF.Ln, bias=epsb[:, 0:1]))(ti),
            reads=(f"vr_{ti % 2}", "eps"), writes=(f"lv{ti % 2}",))
        add("s", (lambda ti: lambda e: e.activation(
            out=invstd[:, ti % 2:ti % 2 + 1], in_=lnvb[:, ti % 2:ti % 2 + 1],
            func=AF.Exp, scale=-0.5))(ti),
            reads=(f"lv{ti % 2}",), writes=(f"iv{ti % 2}",))
        add("v", (lambda ti: lambda e: e.scalar_tensor_tensor(
            out=negmb[:, ti % 2:ti % 2 + 1], in0=meanb[:, ti % 2:ti % 2 + 1],
            scalar=-1.0, in1=invstd[:, ti % 2:ti % 2 + 1],
            op0=ALU.mult, op1=ALU.mult))(ti),
            reads=(f"mn{ti % 2}", f"iv{ti % 2}"), writes=(f"nb{ti % 2}",))
        add("s", (lambda xb, xnb, ti: lambda e: e.activation(
            out=xntm[:, xnb, :], in_=xbuf[:, xb, :], func=AF.Identity,
            bias=negmb[:, ti % 2:ti % 2 + 1],
            scale=invstd[:, ti % 2:ti % 2 + 1]))(xb, xnb, ti),
            reads=(f"xb{xb}", f"nb{ti % 2}", f"iv{ti % 2}"),
            writes=(f"xn{xnb}",))

    def ln_xpose(t):
        ti, xnb = t % HALF_T, t % 2
        for c in range(8):
            pb = P[6 + c // 4]
            add("t", (lambda c, pb, xnb: lambda e: e.matmul(
                pb[:, (c % 4) * 128:(c % 4 + 1) * 128],
                xntm[:, xnb, c * 128:(c + 1) * 128], id_sb[:],
                start=True, stop=True))(c, pb, xnb),
                reads=(f"xn{xnb}", "ident"), writes=(f"P{6 + c // 4}",))
        add("v", (lambda ti: lambda e: e.tensor_copy(
            out=xnfm[:, 0:4, ti * 128:(ti + 1) * 128],
            in_=P[6][:].rearrange("p (c f) -> p c f", f=128)))(ti),
            reads=("P6",), writes=(f"xf{ti}",))
        add("s", (lambda ti: lambda e: e.activation(
            out=xnfm[:, 4:8, ti * 128:(ti + 1) * 128],
            in_=P[7][:].rearrange("p (c f) -> p c f", f=128),
            func=AF.Identity))(ti),
            reads=("P7",), writes=(f"xf{ti}",))

    def phase_a_tile(t):
        ti, xb = t % HALF_T, t % 3
        slot = t % 3
        if (t + 2) % HALF_T > 1 and t + 2 < NT:
            dma_x(t + 2)
        if (t + 1) % HALF_T != 0 and t + 1 < NT:
            ln_block(t + 1)
        for fc in range(16):
            bank = fc // 4
            for d in range(8):
                add("t", (lambda fc, d, bank, ti: lambda e: e.matmul(
                    P[bank][:, (fc % 4) * 128:(fc % 4 + 1) * 128],
                    wq_sb[:, d, fc * 128:(fc + 1) * 128],
                    xnfm[:, d, ti * 128:(ti + 1) * 128],
                    start=(d == 0), stop=(d == 7)))(fc, d, bank, ti),
                    reads=tuple(_wkeys(W_QKV_OFF + d * 6144 + fc * 256, 256))
                    + (f"xf{ti}",),
                    writes=(f"P{bank}",))
            if fc % 4 == 3:
                if fc < 8:
                    add("v", (lambda bank, fc: lambda e: e.tensor_scalar(
                        out=qfm[:, bank * 4:bank * 4 + 4, :],
                        in0=P[bank][:].rearrange("p (c f) -> p c f", f=128),
                        scalar1=0.125, scalar2=None, op0=ALU.mult))(bank, fc),
                        reads=(f"P{bank}",), writes=("qfm",))
                else:
                    kb = bank - 2
                    add("s", (lambda bank, kb, slot: lambda e: e.activation(
                        out=kring[:, kb * 4:kb * 4 + 4,
                                  slot * 128:(slot + 1) * 128],
                        in_=P[bank][:].rearrange("p (c f) -> p c f", f=128),
                        func=AF.Identity))(bank, kb, slot),
                        reads=(f"P{bank}",), writes=(f"kr{slot}",))
        for d in range(8):
            for nb in range(2):
                add("t", (lambda d, nb, ti: lambda e: e.matmul(
                    P[4 + nb][:, :],
                    xnfm[:, d, ti * 128:(ti + 1) * 128],
                    wq_sb[:, d, 2048 + nb * 512:2048 + (nb + 1) * 512],
                    start=(d == 0), stop=(d == 7)))(d, nb, ti),
                    reads=tuple(_wkeys(W_QKV_OFF + d * 6144 + 4096 + nb * 1024,
                                       1024)) + (f"xf{ti}",),
                    writes=(f"P{4 + nb}",))
        for nb in range(2):
            add("s", (lambda nb, slot: lambda e: e.activation(
                out=vring[:, slot, nb * 512:(nb + 1) * 512],
                in_=P[4 + nb][:], func=AF.Identity))(nb, slot),
                reads=(f"P{4 + nb}",), writes=(f"vr{slot}",))
        if (t + 1) % HALF_T != 0 and t + 1 < NT:
            ln_xpose(t + 1)
        NP = NH // 2
        op_sched = {7: (0, 1), 8: (2, 3), 9: (4,), 10: (5,), 11: (6,), 12: (7,)}

        def outproj(hp):
            for nb in range(2):
                add("t", (lambda hp, nb: lambda e: e.matmul(
                    P[2 + nb][:, :], attnfm[:, hp, :],
                    wo_sb[:, hp, nb * 512:(nb + 1) * 512],
                    start=(hp == 0), stop=(hp == 7)))(hp, nb),
                    reads=(f"af{hp}",)
                    + tuple(_wkeys(W_OUT_OFF + hp * 2048 + nb * 1024, 1024)),
                    writes=(f"P{2 + nb}",))

        for s_ in range(NP + 5):
            if 1 <= s_ <= NP:
                pair_softmax(t, 2 * (s_ - 1))
            if s_ < NP:
                pair_scores(t, 2 * s_)
            if 3 <= s_ <= NP + 2:
                pair_transpose(t, 2 * (s_ - 3))
            if 4 <= s_ <= NP + 3:
                pair_av(t, 2 * (s_ - 4))
            for hp in op_sched.get(s_, ()):
                outproj(hp)
        for nb in range(2):
            add("v", (lambda nb, ti, xb: lambda e: e.scalar_tensor_tensor(
                out=yacc[:, ti, nb * 512:(nb + 1) * 512], in0=P[2 + nb][:],
                scalar=1.0, in1=xbuf[:, xb, nb * 512:(nb + 1) * 512],
                op0=ALU.mult, op1=ALU.add))(nb, ti, xb),
                reads=(f"P{2 + nb}", f"xb{xb}"), writes=(f"ya{ti}",))

    def phase_b_parts(hh, ti, last):
        def gate_up(nb):
            gb, ub = nb % 2, 2 + nb % 2
            for d in range(8):
                add("t", (lambda d, nb, ti, gb: lambda e: e.matmul(
                    P[gb][:, :], xnfm[:, d, ti * 128:(ti + 1) * 128],
                    wg_sb[:, d, nb * 512:(nb + 1) * 512],
                    start=(d == 0), stop=(d == 7)))(d, nb, ti, gb),
                    reads=tuple(_wkeys(W_G_OFF + d * 3072 + nb * 1024, 1024))
                    + (f"xf{ti}",),
                    writes=(f"P{gb}",))
                add("t", (lambda d, nb, ti, ub: lambda e: e.matmul(
                    P[ub][:, :], xnfm[:, d, ti * 128:(ti + 1) * 128],
                    wu_sb[:, d, nb * 512:(nb + 1) * 512],
                    start=(d == 0), stop=(d == 7)))(d, nb, ti, ub),
                    reads=tuple(_wkeys(W_U_OFF + d * 3072 + nb * 1024, 1024))
                    + (f"xf{ti}",),
                    writes=(f"P{ub}",))
            add("s", (lambda nb, gb: lambda e: e.activation(
                out=gsb[:, nb * 512:(nb + 1) * 512], in_=P[gb][:],
                func=AF.Silu, scale=1.0 / 127.0))(nb, gb),
                reads=(f"P{gb}",), writes=(f"gs{nb}",))
            add("v", (lambda nb, ub: lambda e: e.tensor_scalar(
                out=usb[:, nb * 512:(nb + 1) * 512], in0=P[ub][:],
                scalar1=1.0 / 127.0, scalar2=None, op0=ALU.mult))(nb, ub),
                reads=(f"P{ub}",), writes=(f"us{nb}",))
            add("v", (lambda nb: lambda e: e.tensor_mul(
                out=hsb[:, nb * 512:(nb + 1) * 512],
                in0=gsb[:, nb * 512:(nb + 1) * 512],
                in1=usb[:, nb * 512:(nb + 1) * 512]))(nb),
                reads=(f"gs{nb}", f"us{nb}"), writes=(f"hs{nb}",))

        def h_trans(nb):
            tb = 4 + nb % 2
            for c4 in range(4):
                c = 4 * nb + c4
                add("t", (lambda c, c4, nb, tb: lambda e: e.matmul(
                    P[tb][:, c4 * 128:(c4 + 1) * 128],
                    hsb[:, c * 128:(c + 1) * 128], id_sb[:],
                    start=True, stop=True))(c, c4, nb, tb),
                    reads=(f"hs{nb}", "ident"), writes=(f"P{tb}",))
            add("s", (lambda nb, tb: lambda e: e.activation(
                out=hfm[:, 4 * nb:4 * nb + 4, :],
                in_=P[tb][:].rearrange("p (c f) -> p c f", f=128),
                func=AF.Identity))(nb, tb),
                reads=(f"P{tb}",), writes=(f"hf{nb}",))

        def down(nb):
            for c4 in range(4):
                c = 4 * nb + c4
                for n2 in range(2):
                    add("t", (lambda c, nb, n2: lambda e: e.matmul(
                        P[6 + n2][:, :], hfm[:, c, :],
                        wd_sb[:, c, n2 * 512:(n2 + 1) * 512],
                        start=(c == 0), stop=(c == 11)))(c, nb, n2),
                        reads=(f"hf{nb}",)
                        + tuple(_wkeys(W_D_OFF + c * 2048 + n2 * 1024, 1024)),
                        writes=(f"P{6 + n2}",))
            if nb == 2:
                for n2 in range(2):
                    if not last:
                        add("v", (lambda n2, ti: lambda e:
                            e.scalar_tensor_tensor(
                                out=yacc[:, ti, n2 * 512:(n2 + 1) * 512],
                                in0=P[6 + n2][:], scalar=1.0 / 127.0,
                                in1=yacc[:, ti, n2 * 512:(n2 + 1) * 512],
                                op0=ALU.mult, op1=ALU.add))(n2, ti),
                            reads=(f"P{6 + n2}", f"ya{ti}"),
                            writes=(f"ya{ti}",))
                    else:
                        yb = ti % 2
                        add("v", (lambda n2, ti, yb: lambda e:
                            e.scalar_tensor_tensor(
                                out=yout[:, yb, n2 * 512:(n2 + 1) * 512],
                                in0=P[6 + n2][:], scalar=1.0 / 127.0,
                                in1=yacc[:, ti, n2 * 512:(n2 + 1) * 512],
                                op0=ALU.mult, op1=ALU.add))(n2, ti, yb),
                            reads=(f"P{6 + n2}", f"ya{ti}"),
                            writes=(f"yo{yb}",))
                if last:
                    t = _half[0] * HALF_T + ti
                    yb = ti % 2
                    add("q", (lambda t, yb: lambda e: e.dma_start(
                        out=y_d[t * 128:(t + 1) * 128, :],
                        in_=yout[:, yb, :]))(t, yb),
                        reads=(f"yo{yb}",), dma=f"dy{yb}")

        return gate_up, h_trans, down

    def phase_b(hh, last):
        nsteps = 3 * HALF_T
        parts = [phase_b_parts(hh, ti, last) for ti in range(HALF_T)]
        for st_ in range(nsteps + 2):
            if st_ < nsteps:
                ti, nb = divmod(st_, 3)
                parts[ti][0](nb)
            if st_ >= 1 and st_ - 1 < nsteps:
                ti, nb = divmod(st_ - 1, 3)
                parts[ti][1](nb)
            if st_ >= 2 and st_ - 2 < nsteps:
                ti, nb = divmod(st_ - 2, 3)
                parts[ti][2](nb)

    _half = [0]
    for H in range(2):
        _half[0] = H
        load_attn_weights()
        dma_x(H * HALF_T)
        dma_x(H * HALF_T + 1)
        ln_block(H * HALF_T)
        ln_xpose(H * HALF_T)
        for ti in range(HALF_T):
            phase_a_tile(H * HALF_T + ti)
        for hh in range(2):
            load_ffn_weights(hh)
            phase_b(hh, last=(hh == 1))

    from contextlib import ExitStack
    with ExitStack() as ctx:
        sems = {}
        for name in ["t", "v", "s", "g", "dc", "dx0", "dx1", "dx2",
                     "dwa0", "dwa1", "dwo",
                     "dwg0", "dwg1", "dwg2", "dwg3",
                     "dwg4", "dwg5", "dwg6", "dwg7",
                     "dwu0", "dwu1", "dwu2", "dwu3",
                     "dwu4", "dwu5", "dwu6", "dwu7",
                     "dwd0", "dwd1", "dy0", "dy1"]:
            sems[name] = ctx.enter_context(nc.semaphore(name))
        block = ctx.enter_context(nc.Block())

        def emit(ename):
            def run(eng):
                from concourse.bass import inst_waits_full
                for waits, fn, sem, amt in sc.ops[ename]:
                    att = waits[-1:] if waits else []
                    for s_, v_ in (waits[:-1] if att else waits):
                        eng.wait_ge(sems[s_], v_)
                    ins = fn(eng)
                    for s_, v_ in att:
                        if inst_waits_full(ins.ins):
                            raise RuntimeError("sync header full")
                        ins._wait_ge(sems[s_], v_)
                    ins.then_inc(sems[sem], amt)
                if ename == "q":
                    eng.wait_ge(sems["dy0"], sc.cnt["dy0"])
                    eng.wait_ge(sems["dy1"], sc.cnt["dy1"])
            return run

        block.tensor(emit("t"))
        block.vector(emit("v"))
        block.scalar(emit("s"))
        block.gpsimd(emit("g"))
        block.sync(emit("q"))
    return nc


def _consts():
    ident = np.eye(TILE, dtype=ml_dtypes.bfloat16)
    p = np.arange(TILE)[:, None]
    c = np.arange(TILE)[None, :]
    neg = np.float32(-1000.0)
    m_full = np.full((TILE, TILE), neg, dtype=np.float32)
    m_a = np.where(c <= p, neg, 0.0).astype(np.float32)
    m_c = np.where(c > p, neg, 0.0).astype(np.float32)
    masks = np.stack([m_full.T, m_a.T, m_c.T], axis=0)
    masks = np.ascontiguousarray(
        masks.transpose(1, 0, 2)).astype(ml_dtypes.bfloat16)
    return ident, masks


_NC_CACHE = {}


def kernel(L0_attn_x, L0_attn_w_qkv, L0_attn_w_out, L0_ffn_w_gate,
           L0_ffn_w_up, L0_ffn_w_down):
    x = np.ascontiguousarray(np.asarray(L0_attn_x), dtype=np.float32)
    wq = np.ascontiguousarray(np.asarray(L0_attn_w_qkv), dtype=np.float32)
    wo = np.ascontiguousarray(np.asarray(L0_attn_w_out), dtype=np.float32)
    wg = np.ascontiguousarray(np.asarray(L0_ffn_w_gate).T).astype(np.int32)
    wu = np.ascontiguousarray(np.asarray(L0_ffn_w_up).T).astype(np.int32)
    wd = np.ascontiguousarray(np.asarray(L0_ffn_w_down).T).astype(np.int32)
    ident, masks = _consts()
    if "nc" not in _NC_CACHE:
        _NC_CACHE["nc"] = build_program()
    nc = _NC_CACHE["nc"]
    in_maps = []
    for cidx in range(8):
        in_maps.append({
            "x": np.ascontiguousarray(x[cidx]),
            "wq": wq, "wo": wo, "wg": wg, "wu": wu, "wd": wd,
            "ident": ident, "masks": masks,
        })
    res = run_bass_kernel_spmd(nc, in_maps, core_ids=list(range(8)))
    out = np.stack([res.results[cidx]["y"] for cidx in range(8)], axis=0)
    return out
